# version 01
# baseline (speedup 1.0000x reference)
import contextlib
import numpy as np
import concourse.bass as bass
import concourse.mybir as mybir
from concourse.bass_utils import run_bass_kernel_spmd

F32 = mybir.dt.float32
BF16 = mybir.dt.bfloat16
ALU = mybir.AluOpType
ACT = mybir.ActivationFunctionType
AX = mybir.AxisListType

D = 2048
DC = 16
T = 1024
H = 5461
HC = 43
NH = 16
S = 4096
NW = float(H * D)
INVSQ = float(1.0 / np.sqrt(128.0))

def hsz(hc):
    return H - 128 * (HC - 1) if hc == HC - 1 else 128

SEMS = ['pe', 'act', 'dve', 'gpe', 'ds', 'dg', 'col', 's0', 's1', 's2', 's3', 'k0', 'k1', 'g0', 'g1', 'q0', 'q1', 'v0', 'v1']


def build_nc():
    nc = bass.Bass("TRN2", target_bir_lowering=False)
    xT = nc.dram_tensor("xT", [D, T], F32, kind="ExternalInput")
    w1T = nc.dram_tensor("w1T", [D, H], F32, kind="ExternalInput")
    w3T = nc.dram_tensor("w3T", [D, H], F32, kind="ExternalInput")
    w2T = nc.dram_tensor("w2T", [H, D], F32, kind="ExternalInput")
    wqkvT = nc.dram_tensor("wqkvT", [D, 3 * D], F32, kind="ExternalInput")
    wprojT = nc.dram_tensor("wprojT", [D, D], F32, kind="ExternalInput")
    bprojI = nc.dram_tensor("bprojI", [1, D], F32, kind="ExternalInput")
    sl1 = nc.dram_tensor("sl1", [256, H], F32, kind="ExternalInput")
    sl3 = nc.dram_tensor("sl3", [256, H], F32, kind="ExternalInput")
    sl2 = nc.dram_tensor("sl2", [768, D], F32, kind="ExternalInput")
    isupI = nc.dram_tensor("isupI", [1, 2], F32, kind="ExternalInput")
    bselI = nc.dram_tensor("bselI", [16, 16 * 128], F32, kind="ExternalInput")
    outT = nc.dram_tensor("outT", [D, T], F32, kind="ExternalOutput")
    q1s = nc.dram_tensor("q1s", [D, H], BF16)
    q3s = nc.dram_tensor("q3s", [D, H], BF16)
    q2s = nc.dram_tensor("q2s", [H, D], BF16)
    ag1i = nc.dram_tensor("ag1i", [S, T], BF16)
    ag1o = nc.dram_tensor("ag1o", [4 * S, T], BF16)
    ag2i = nc.dram_tensor("ag2i", [T, T], BF16)
    ag2o = nc.dram_tensor("ag2o", [2 * T, T], BF16)
    ari = nc.dram_tensor("ari", [1, 4], F32)
    aro = nc.dram_tensor("aro", [1, 4], F32)
    RG4 = [[0, 1, 2, 3], [4, 5, 6, 7]]
    RG2 = [[0, 4], [1, 5], [2, 6], [3, 7]]
    RG8 = [list(range(8))]

    EV = []
    CNT = {s: 0 for s in SEMS}
    BAR = [()]
    UID = [0]

    def U(n):
        UID[0] += 1
        return f"{n}_{UID[0]}" 

    def ev(eng, fn, waits=(), inc=None, amt=1):
        EV.append((eng, BAR[0] + tuple(waits), inc, amt if inc else 0, fn))
        if inc:
            CNT[inc] += amt

    def Wl(*ss):
        return tuple((s, CNT[s]) for s in ss)

    def barrier():
        BAR[0] = Wl('pe', 'act', 'dve', 'gpe', 'ds', 'dg', 'col', 's0', 's1', 's2', 's3', 'k0', 'k1', 'g0', 'g1', 'q0', 'q1', 'v0', 'v1')

    def dma(q, o, i, semn, waits=()):
        ev(q, (lambda e, o=o, i=i: e.dma_start(out=o, in_=i)), waits, semn, 16)

    def drain(eng):
        ev(eng, lambda e: e.drain())

    def mm(ps, lhsT, rhs, start, stop, waits=()):
        ev('tensor', (lambda e, ps=ps, l=lhsT, r=rhs, a=start, b=stop:
                      e.matmul(ps, l, r, start=a, stop=b)), waits, 'pe', 1)

    with contextlib.ExitStack() as ST:
        E = ST.enter_context
        x = E(nc.sbuf_tensor("x", [128, DC * T], F32))
        ones_c_b = E(nc.sbuf_tensor("ones_c_b", [128, 1], BF16))
        ones_r_f = E(nc.sbuf_tensor("ones_r_f", [1, 128], F32))
        selb = E(nc.sbuf_tensor("selb", [128, NH * NH], BF16))
        b20v = E(nc.sbuf_tensor("b20v", [128, 1], F32))
        isqv = E(nc.sbuf_tensor("isqv", [128, 1], F32))
        epsr = E(nc.sbuf_tensor("epsr", [1, 1], F32))
        arsb = E(nc.sbuf_tensor("arsb", [1, 4], F32))
        throw = E(nc.sbuf_tensor("throw", [1, 8], F32))
        thrv = E(nc.sbuf_tensor("thrv", [128, 8], F32))
        uv = E(nc.sbuf_tensor("uv", [128, 2], F32))
        isup_sb = E(nc.sbuf_tensor("isup_sb", [1, 2], F32))
        bpv = E(nc.sbuf_tensor("bpv", [128, DC], F32))
        onesf = E(nc.sbuf_tensor("onesf", [128, 1], F32))
        sem = {s: E(nc.semaphore(s)) for s in SEMS}

        for dc in range(DC):
            dma('sync', x[:, dc * T:(dc + 1) * T], xT[dc * 128:(dc + 1) * 128, :], 'ds')
        for fn in [lambda e: e.memset(ones_c_b[:, :], 1.0),
                   lambda e: e.memset(ones_r_f[0:1, :], 1.0),
                   lambda e: e.memset(onesf[:, :], 1.0),
                   lambda e: e.memset(b20v[:, :], -20.0),
                   lambda e: e.memset(isqv[:, :], INVSQ),
                   lambda e: e.memset(epsr[0:1, :], 1e-5),
                   lambda e: e.memset(selb[:, :], 0.0)]:
            ev('gpsimd', fn, (), 'gpe', 1)
        zdone = Wl('gpe')
        for h in range(NH):
            ev('gpsimd', lambda e, h=h: e.memset(selb[:, h * NH + h:h * NH + h + 1], 1.0),
               zdone, 'gpe', 1)
        for dc in range(DC):
            dma('gpsimd', bpv[:, dc:dc + 1],
                bprojI[0:1, dc * 128:(dc + 1) * 128].rearrange("a p -> p a"), 'k0')
        dma('gpsimd', isup_sb[0:1, :], isupI[0:1, :], 'k0')

        with contextlib.ExitStack() as SC, contextlib.ExitStack() as PC:
            E0, P0 = SC.enter_context, PC.enter_context
            stg = [E0(nc.sbuf_tensor(f"stg{i}", [128, H], F32)) for i in range(2)]
            pa = E0(nc.sbuf_tensor("pa", [128, 10], F32))
            pas = E0(nc.sbuf_tensor("pas", [128, 3], F32))
            psS = P0(nc.psum_tensor("psS", [128, 512], F32))
            loads = ([(sl1, i * 128, H, i) for i in range(2)] +
                     [(sl3, i * 128, H, 2 + i) for i in range(2)] +
                     [(sl2, i * 128, D, 4 + i) for i in range(6)])
            free_at = [(), ()]
            for src, r0, w, col in loads:
                slot = col % 2
                ssem = 's0' if slot == 0 else 's1'
                dma('sync', stg[slot][:, 0:w], src[r0:r0 + 128, 0:w], ssem, waits=free_at[slot])
                ev('vector', lambda e, c=col, s_=slot, w=w:
                   e.reduce_sum(pa[:, c:c + 1], stg[s_][:, 0:w], axis=AX.X,
                                apply_absolute_value=True), Wl(ssem), 'dve', 1)
                free_at[slot] = Wl('dve')
            drain('vector')
            ev('vector', lambda e: e.reduce_sum(pas[:, 0:1], pa[:, 0:2], axis=AX.X), (), 'dve', 1)
            ev('vector', lambda e: e.reduce_sum(pas[:, 1:2], pa[:, 2:4], axis=AX.X), (), 'dve', 1)
            ev('vector', lambda e: e.reduce_sum(pas[:, 2:3], pa[:, 4:10], axis=AX.X), (), 'dve', 1)
            drain('vector')
            pvw = Wl('dve', 'gpe')
            for j in range(3):
                mm(psS[0:1, j:j + 1], onesf[:, :], pas[:, j:j + 1], True, True, waits=pvw)
            ev('vector', lambda e: e.tensor_copy(arsb[0:1, 0:3], psS[0:1, 0:3]), Wl('pe'), 'dve', 1)
            ev('vector', lambda e: e.memset(arsb[0:1, 3:4], 0.0), (), 'dve', 1)
            dma('gpsimd', ari[0:1, :], arsb[0:1, :], 'dg', waits=Wl('dve'))
            ev('gpsimd', lambda e: e.collective_compute(
                "AllReduce", ALU.add, replica_groups=RG8, ins=[ari[:, :]], outs=[aro[:, :]]),
                Wl('dg'), 'col', 1)
            dma('gpsimd', arsb[0:1, :], aro[0:1, :], 'dg', waits=Wl('col'))
            arw = Wl('dg')
            for j, (c0, fac) in enumerate([(0, 0.5 / NW), (0, -0.5 / NW), (1, 0.5 / NW),
                                           (1, -0.5 / NW), (2, 0.5 / NW), (2, -0.5 / NW),
                                           (0, 1.0 / NW)]):
                ev('vector', lambda e, j=j, c0=c0, fac=fac:
                   e.tensor_scalar(throw[0:1, j:j + 1], arsb[0:1, c0:c0 + 1], fac, None, ALU.mult),
                   arw, 'dve', 1)
            ev('vector', lambda e: e.scalar_tensor_tensor(
                throw[0:1, 7:8], arsb[0:1, 1:2], 0.5 / (NW * NW), arsb[0:1, 2:3],
                ALU.mult, ALU.mult), arw, 'dve', 1)
            drain('vector')
            tw = Wl('dve')
            mm(psS[0:128, 0:8], ones_r_f[0:1, :], throw[0:1, 0:8], True, True, waits=tw)
            mm(psS[0:128, 8:10], ones_r_f[0:1, :], isup_sb[0:1, 0:2], True, True,
               waits=tw + Wl('k0'))
            ev('vector', lambda e: e.tensor_copy(thrv[:, :], psS[:, 0:8]), Wl('pe'), 'dve', 1)
            ev('vector', lambda e: e.tensor_copy(uv[:, :], psS[:, 8:10]), Wl('pe'), 'dve', 1)
        barrier()

        def ln_phase(xn):
            with contextlib.ExitStack() as SC, contextlib.ExitStack() as PC:
                E0, P0 = SC.enter_context, PC.enter_context
                xc = [E0(nc.sbuf_tensor(U(f"lnxc{i}"), [128, T], BF16)) for i in range(2)]
                sq = [E0(nc.sbuf_tensor(U(f"lnsq{i}"), [128, T], BF16)) for i in range(2)]
                tmp = [E0(nc.sbuf_tensor(U(f"lntmp{i}"), [128, T], F32)) for i in range(2)]
                rowm = E0(nc.sbuf_tensor(U("rowm"), [1, T], F32))
                rowv = E0(nc.sbuf_tensor(U("rowv"), [1, T], F32))
                rowt = E0(nc.sbuf_tensor(U("rowt"), [1, T], F32))
                rowA = E0(nc.sbuf_tensor(U("rowA"), [1, T], F32))
                rowB = E0(nc.sbuf_tensor(U("rowB"), [1, T], F32))
                ps_sx = P0(nc.psum_tensor(U("ps_sx"), [128, T], F32))
                ps_s2 = P0(nc.psum_tensor(U("ps_s2"), [128, T], F32))
                bcA = P0(nc.psum_tensor(U("bcA"), [128, T], F32))
                bcB = P0(nc.psum_tensor(U("bcB"), [128, T], F32))
                xc_free = [(), ()]
                sq_free = [(), ()]
                for dc in range(DC):
                    s_ = dc % 2
                    ev('vector', lambda e, dc=dc, s=s_: e.tensor_copy(
                        xc[s][:, :], x[:, dc * T:(dc + 1) * T]), xc_free[s_], 'dve', 1)
                    cw = Wl('dve')
                    ev('scalar', lambda e, s=s_: e.activation(
                        out=sq[s][:, :], in_=xc[s][:, :], func=ACT.Square, bias=0.0, scale=1.0),
                        cw + sq_free[s_], 'act', 1)
                    aw = Wl('act')
                    for half in range(2):
                        mm(ps_sx[0:1, half * 512:(half + 1) * 512], ones_c_b[:, :],
                           xc[s_][:, half * 512:(half + 1) * 512], dc == 0, dc == DC - 1, waits=cw)
                        mm(ps_s2[0:1, half * 512:(half + 1) * 512], ones_c_b[:, :],
                           sq[s_][:, half * 512:(half + 1) * 512], dc == 0, dc == DC - 1, waits=aw)
                    xc_free[s_] = Wl('pe') + aw
                    sq_free[s_] = Wl('pe')
                statw = Wl('pe')
                ev('vector', lambda e: e.tensor_scalar(rowm[0:1, :], ps_sx[0:1, :],
                   1.0 / D, None, ALU.mult), statw, 'dve', 1)
                ev('scalar', lambda e: e.activation(out=rowt[0:1, :], in_=rowm[0:1, :],
                   func=ACT.Square, bias=0.0, scale=1.0), Wl('dve'), 'act', 1)
                ev('vector', lambda e: e.scalar_tensor_tensor(rowv[0:1, :], ps_s2[0:1, :],
                   1.0 / D, rowt[0:1, :], ALU.mult, ALU.subtract), Wl('act'), 'dve', 1)
                ev('scalar', lambda e: e.activation(out=rowt[0:1, :], in_=rowv[0:1, :],
                   func=ACT.Sqrt, bias=epsr[0:1, :], scale=1.0), Wl('dve'), 'act', 1)
                ev('vector', lambda e: e.reciprocal(rowA[0:1, :], rowt[0:1, :]), Wl('act'), 'dve', 1)
                drain('vector')
                ev('vector', lambda e: e.scalar_tensor_tensor(rowB[0:1, :], rowm[0:1, :],
                   -1.0, rowA[0:1, :], ALU.mult, ALU.mult), (), 'dve', 1)
                drain('vector')
                rw = Wl('dve')
                for half in range(2):
                    mm(bcA[:, half * 512:(half + 1) * 512], ones_r_f[0:1, :],
                       rowA[0:1, half * 512:(half + 1) * 512], True, True, waits=rw)
                    mm(bcB[:, half * 512:(half + 1) * 512], ones_r_f[0:1, :],
                       rowB[0:1, half * 512:(half + 1) * 512], True, True, waits=rw)
                bw = Wl('pe')
                tmp_free = [(), ()]
                for dc in range(DC):
                    s_ = dc % 2
                    ev('vector', lambda e, dc=dc, s=s_: e.tensor_tensor(
                        tmp[s][:, :], x[:, dc * T:(dc + 1) * T], bcA[:, :], ALU.mult),
                        bw + tmp_free[s_], 'dve', 1)
                    drain('vector')
                    ev('vector', lambda e, dc=dc, s=s_: e.tensor_tensor(
                        xn[:, dc * T:(dc + 1) * T], tmp[s][:, :], bcB[:, :], ALU.add),
                        (), 'dve', 1)
                    tmp_free[s_] = Wl('dve')
            barrier()

        def quant_phase(srcT, dstS, rows, cols, cp, cn):
            ntile = (rows + 127) // 128
            with contextlib.ExitStack() as SC:
                E0 = SC.enter_context
                stg = [E0(nc.sbuf_tensor(U(f"qstg{i}"), [128, cols], F32)) for i in range(2)]
                qb = E0(nc.sbuf_tensor(U("qqb"), [128, cols], BF16))
                qr = [E0(nc.sbuf_tensor(U(f"qqr{i}"), [128, cols], BF16)) for i in range(2)]
                free_at = [(), ()]
                wr_at = [(), ()]
                for i in range(ntile):
                    s_ = i % 2
                    ssem = 's0' if s_ == 0 else 's1'
                    r0 = i * 128
                    rh = min(128, rows - r0)
                    dma('sync', stg[s_][0:rh, :], srcT[r0:r0 + rh, :], ssem, waits=free_at[s_])
                    lw = Wl(ssem)
                    ev('vector', lambda e, s=s_, rh=rh: e.tensor_scalar(
                        qb[0:rh, :], stg[s][0:rh, :], thrv[0:rh, cn:cn + 1], None, ALU.is_le),
                        lw + wr_at[s_], 'dve', 1)
                    drain('vector')
                    ev('vector', lambda e, s=s_, rh=rh: e.scalar_tensor_tensor(
                        qr[s][0:rh, :], stg[s][0:rh, :], thrv[0:rh, cp:cp + 1], qb[0:rh, :],
                        ALU.is_ge, ALU.subtract), (), 'dve', 1)
                    drain('vector')
                    free_at[s_] = Wl('dve')
                    wsem = 'g0' if s_ == 0 else 'g1'
                    dma('gpsimd', dstS[r0:r0 + rh, :], qr[s_][0:rh, :], wsem, waits=Wl('dve'))
                    wr_at[s_] = Wl(wsem)
            barrier()

        def ffn_pass(xn):
          for _hl, _hh in ((0, 22), (22, HC)):
            ffn_half(xn, _hl, _hh)

        def ffn_half(xn, hc_lo, hc_hi):
            NBL = (hc_hi - hc_lo + 1) // 2
            with contextlib.ExitStack() as SG:
                g = SG.enter_context(nc.sbuf_tensor(U("g"), [128, 22 * T], BF16))
                with contextlib.ExitStack() as SC, contextlib.ExitStack() as PC:
                    E0, P0 = SC.enter_context, PC.enter_context
                    ring = [E0(nc.sbuf_tensor(U(f"ring{i}"), [128, DC * 256], BF16)) for i in range(3)]
                    sis = E0(nc.sbuf_tensor(U("sis"), [128, T], BF16))
                    h1p = [P0(nc.psum_tensor(U(f"h1p{j}"), [128, T], F32)) for j in range(2)]
                    h3p = [P0(nc.psum_tensor(U(f"h3p{j}"), [128, T], F32)) for j in range(2)]
                    blk_free = [(), (), ()]
                    h1_free = [(), ()]
                    h3_free = [(), ()]
                    sis_free = ()
                    rsem = ['s0', 's1', 's2']
                    for bi in range(NBL):
                        r1, r3 = (2 * bi) % 3, (2 * bi + 1) % 3
                        hcs = [hc for hc in (hc_lo + 2 * bi, hc_lo + 2 * bi + 1) if hc < hc_hi]
                        c0 = hcs[0] * 128
                        cw_ = sum(hsz(h) for h in hcs)
                        for dc in range(DC):
                            dma('sync', ring[r1][:, dc * 256:dc * 256 + cw_],
                                q1s[dc * 128:(dc + 1) * 128, c0:c0 + cw_],
                                rsem[r1], waits=blk_free[r1] if dc == 0 else ())
                            dma('sync', ring[r3][:, dc * 256:dc * 256 + cw_],
                                q3s[dc * 128:(dc + 1) * 128, c0:c0 + cw_],
                                rsem[r3], waits=blk_free[r3] if dc == 0 else ())
                        lw = Wl(rsem[r1], rsem[r3])
                        for dc in range(DC):
                            for j, hc in enumerate(hcs):
                                co = dc * 256 + (hsz(hcs[0]) if j else 0)
                                sz = hsz(hc)
                                for half in range(2):
                                    rhs = xn[:, dc * T + half * 512:dc * T + (half + 1) * 512]
                                    mm(h1p[j][0:sz, half * 512:(half + 1) * 512],
                                       ring[r1][:, co:co + sz], rhs, dc == 0, dc == DC - 1,
                                       waits=lw + (h1_free[j] if dc == 0 else ()))
                                    mm(h3p[j][0:sz, half * 512:(half + 1) * 512],
                                       ring[r3][:, co:co + sz], rhs, dc == 0, dc == DC - 1,
                                       waits=lw + (h3_free[j] if dc == 0 else ()))
                        mmw = Wl('pe')
                        for j, hc in enumerate(hcs):
                            sz = hsz(hc)
                            ev('scalar', lambda e, j=j, sz=sz: e.activation(
                                out=sis[0:sz, :], in_=h1p[j][0:sz, :], func=ACT.Silu,
                                bias=0.0, scale=thrv[0:sz, 6:7]), mmw + sis_free, 'act', 1)
                            h1_free[j] = Wl('act')
                            ev('vector', lambda e, j=j, gl=hc - hc_lo, sz=sz: e.tensor_tensor(
                                g[0:sz, gl * T:(gl + 1) * T], sis[0:sz, :], h3p[j][0:sz, :],
                                ALU.mult), Wl('act'), 'dve', 1)
                            h3_free[j] = Wl('dve')
                            sis_free = Wl('dve')
                        blk_free[r1] = Wl('pe')
                        blk_free[r3] = Wl('pe')
                    gw = Wl('dve')
                with contextlib.ExitStack() as SC2, contextlib.ExitStack() as PC2:
                    E2, P2 = SC2.enter_context, PC2.enter_context
                    q2b = [E2(nc.sbuf_tensor(U(f"q2b{i}"), [128, 512], BF16)) for i in range(2)]
                    yp = [P2(nc.psum_tensor(U(f"yp{j}"), [128, T], F32)) for j in range(4)]
                    q2_free = [(), ()]
                    yfree = [(), (), (), ()]
                    for dmq in range(4):
                        for hc in range(hc_lo, hc_hi):
                            s_ = hc % 2
                            ssem = 's0' if s_ == 0 else 's1'
                            sz = hsz(hc)
                            gl = hc - hc_lo
                            dma('sync', q2b[s_][0:sz, :],
                                q2s[hc * 128:hc * 128 + sz, dmq * 512:(dmq + 1) * 512],
                                ssem, waits=q2_free[s_] + gw)
                            lw2 = Wl(ssem)
                            for j in range(4):
                                for half in range(2):
                                    mm(yp[j][:, half * 512:(half + 1) * 512],
                                       q2b[s_][0:sz, j * 128:(j + 1) * 128],
                                       g[0:sz, gl * T + half * 512:gl * T + (half + 1) * 512],
                                       hc == hc_lo, hc == hc_hi - 1,
                                       waits=lw2 + (yfree[j] if hc == hc_lo else ()))
                            q2_free[s_] = Wl('pe')
                        ew = Wl('pe')
                        for j in range(4):
                            dmc = dmq * 4 + j
                            ev('vector', lambda e, j=j, dmc=dmc: e.scalar_tensor_tensor(
                                x[:, dmc * T:(dmc + 1) * T], yp[j][:, :], thrv[:, 7:8],
                                x[:, dmc * T:(dmc + 1) * T], ALU.mult, ALU.add), ew, 'dve', 1)
                            yfree[j] = Wl('dve')
            barrier()

        with contextlib.ExitStack() as SX:
            xn1 = SX.enter_context(nc.sbuf_tensor(U("xn1"), [128, DC * T], BF16))
            ln_phase(xn1)
            quant_phase(w1T, q1s, D, H, 0, 1)
            quant_phase(w3T, q3s, D, H, 2, 3)
            quant_phase(w2T, q2s, H, D, 4, 5)
            ffn_pass(xn1)

        with contextlib.ExitStack() as SA:
            EA = SA.enter_context
            q_all = EA(nc.sbuf_tensor("q_all", [128, NH * T], BF16))
            with contextlib.ExitStack() as SX:
                xn2 = SX.enter_context(nc.sbuf_tensor(U("xn2"), [128, DC * T], BF16))
                ln_phase(xn2)
                with contextlib.ExitStack() as SQ, contextlib.ExitStack() as PQ:
                    EQ, PP = SQ.enter_context, PQ.enter_context
                    qkw = [EQ(nc.sbuf_tensor(f"qkw{i}", [128, DC * 256], BF16)) for i in range(2)]
                    kst = [EQ(nc.sbuf_tensor(f"kst{i}", [128, T], BF16)) for i in range(2)]
                    qp = [PP(nc.psum_tensor(f"qp{i}", [128, T], F32)) for i in range(2)]
                    kp = [PP(nc.psum_tensor(f"kp{i}", [128, T], F32)) for i in range(2)]
                    qkw_free = [(), ()]
                    qp_free = [(), ()]
                    kp_free = [(), ()]
                    kst_free = [(), ()]
                    for h in range(NH):
                        s_ = h % 2
                        qsem = 'q0' if s_ == 0 else 'q1'
                        for dc in range(DC):
                            dma('gpsimd', qkw[s_][:, dc * 256:(dc + 1) * 256],
                                wqkvT[dc * 128:(dc + 1) * 128, h * 384:h * 384 + 256],
                                qsem, waits=qkw_free[s_] if dc == 0 else ())
                        lw = Wl(qsem)
                        for dc in range(DC):
                            for half in range(2):
                                rhs = xn2[:, dc * T + half * 512:dc * T + (half + 1) * 512]
                                mm(qp[s_][:, half * 512:(half + 1) * 512],
                                   qkw[s_][:, dc * 256:dc * 256 + 128], rhs,
                                   dc == 0, dc == DC - 1,
                                   waits=lw + (qp_free[s_] if dc == 0 else ()))
                                mm(kp[s_][:, half * 512:(half + 1) * 512],
                                   qkw[s_][:, dc * 256 + 128:dc * 256 + 256], rhs,
                                   dc == 0, dc == DC - 1,
                                   waits=lw + (kp_free[s_] if dc == 0 else ()))
                        qkw_free[s_] = Wl('pe')
                        mmw = Wl('pe')
                        ev('scalar', lambda e, h=h, s=s_: e.activation(
                            out=q_all[:, h * T:(h + 1) * T], in_=qp[s][:, :], func=ACT.Copy,
                            bias=0.0, scale=isqv[:, :]), mmw, 'act', 1)
                        qp_free[s_] = Wl('act')
                        ev('scalar', lambda e, s=s_: e.activation(
                            out=kst[s][:, :], in_=kp[s][:, :], func=ACT.Copy,
                            bias=0.0, scale=1.0), mmw + kst_free[s_], 'act', 1)
                        kp_free[s_] = Wl('act')
                        ksem = 'k0' if s_ == 0 else 'k1'
                        dma('gpsimd', ag1i[h * 128:(h + 1) * 128, :], kst[s_][:, :], ksem,
                            waits=Wl('act'))
                        kst_free[s_] = Wl(ksem)
                barrier()
                ag1v = ag1i[D:S, :].rearrange("(a b) c -> a (b c)", b=2)
                with contextlib.ExitStack() as SV, contextlib.ExitStack() as PV:
                    EQ, PP = SV.enter_context, PV.enter_context
                    vw = [EQ(nc.sbuf_tensor(f"vw{i}", [128, DC * 512], BF16)) for i in range(2)]
                    vst = [EQ(nc.sbuf_tensor(f"vst{i}", [128, 512], BF16)) for i in range(2)]
                    vp = [PP(nc.psum_tensor(f"vp{t}", [128, 512], F32)) for t in range(8)]
                    vw_free = [(), ()]
                    vp_free = [()] * 8
                    vst_free = [(), ()]
                    for qd in range(4):
                        s_ = qd % 2
                        vsem = 'v0' if s_ == 0 else 'v1'
                        for j in range(4):
                            col0 = (4 * qd + j) * 384 + 256
                            for dc in range(DC):
                                dma('gpsimd', vw[s_][:, dc * 512 + j * 128:dc * 512 + (j + 1) * 128],
                                    wqkvT[dc * 128:(dc + 1) * 128, col0:col0 + 128],
                                    vsem, waits=vw_free[s_] if (dc == 0 and j == 0) else ())
                        lw = Wl(vsem)
                        for dc in range(DC):
                            for tc in range(8):
                                mm(vp[tc][:, :], xn2[:, dc * T + tc * 128:dc * T + (tc + 1) * 128],
                                   vw[s_][:, dc * 512:(dc + 1) * 512], dc == 0, dc == DC - 1,
                                   waits=lw + (vp_free[tc] if dc == 0 else ()))
                        vw_free[s_] = Wl('pe')
                        mmw = Wl('pe')
                        for tc in range(8):
                            ssl = tc % 2
                            ev('scalar', lambda e, tc=tc, s=ssl: e.activation(
                                out=vst[s][:, :], in_=vp[tc][:, :], func=ACT.Copy,
                                bias=0.0, scale=1.0), mmw + vst_free[ssl], 'act', 1)
                            vp_free[tc] = Wl('act')
                            wsem = 'g0' if ssl == 0 else 'g1'
                            dma('gpsimd', ag1v[tc * 128:(tc + 1) * 128, qd * 512:(qd + 1) * 512],
                                vst[ssl][:, :], wsem, waits=Wl('act'))
                            vst_free[ssl] = Wl(wsem)
            ev('gpsimd', lambda e: e.collective_compute(
                "AllGather", ALU.bypass, replica_groups=RG4, ins=[ag1i[:, :]], outs=[ag1o[:, :]]),
                Wl('k0', 'k1', 'g0', 'g1', 'gpe'), 'col', 1)
            ag1w = Wl('col')
            barrier()

            ctx = EA(nc.sbuf_tensor("ctx", [128, NH * T], BF16))
            zsb = EA(nc.sbuf_tensor("zsb", [16, T], F32))
            zin = EA(nc.sbuf_tensor("zin", [16, T], F32))
            with contextlib.ExitStack() as SK, contextlib.ExitStack() as PK:
                EK, PP = SK.enter_context, PK.enter_context
                kh = [EK(nc.sbuf_tensor(f"kh{i}", [128, S], BF16)) for i in range(2)]
                vq = EK(nc.sbuf_tensor("vq", [128, 32 * 512], BF16))
                pT = [EK(nc.sbuf_tensor(f"pT{i}", [128, T], BF16)) for i in range(2)]
                sp = [PP(nc.psum_tensor(f"sp{i}", [128, T], F32)) for i in range(2)]
                cxp = PP(nc.psum_tensor("cxp", [128, T], F32))
                zp = [PP(nc.psum_tensor(f"zp{i}", [128, 512], F32)) for i in range(2)]
                kh_free = [(), ()]
                vq_free = ()
                sp_free = [(), ()]
                cxp_free = ()
                pT_free = [(), ()]
                vqw = ()
                for h in range(NH):
                    khs = h % 2
                    if h % 4 == 0:
                        for tkc in range(32):
                            r, tl = tkc // 8, tkc % 8
                            src = ag1o[r * S + D:(r + 1) * S, :].rearrange("(a b) c -> a (b c)", b=2)
                            dma('sync', vq[:, tkc * 512:(tkc + 1) * 512],
                                src[tl * 128:(tl + 1) * 128, (h // 4) * 512:(h // 4 + 1) * 512],
                                's2', waits=ag1w + vq_free)
                        vqw = Wl('s2')
                    khsem = 's0' if khs == 0 else 's1'
                    for r in range(4):
                        dma('sync', kh[khs][:, r * T:(r + 1) * T],
                            ag1o[r * S + h * 128:r * S + (h + 1) * 128, :], khsem,
                            waits=ag1w + kh_free[khs])
                    khw = Wl(khsem)
                    for tkc in range(32):
                        par = tkc % 2
                        for half in range(2):
                            mm(sp[par][:, half * 512:(half + 1) * 512],
                               kh[khs][:, tkc * 128:(tkc + 1) * 128],
                               q_all[:, h * T + half * 512:h * T + (half + 1) * 512],
                               True, True, waits=khw + sp_free[par])
                        spw = Wl('pe')
                        ev('scalar', lambda e, p=par: e.activation(
                            out=pT[p][:, :], in_=sp[p][:, :], func=ACT.Exp,
                            bias=b20v[:, :], scale=1.0), spw + pT_free[par], 'act', 1)
                        expw = Wl('act')
                        sp_free[par] = expw
                        for half in range(2):
                            mm(cxp[:, half * 512:(half + 1) * 512],
                               vq[:, tkc * 512 + (h % 4) * 128:tkc * 512 + (h % 4 + 1) * 128],
                               pT[par][:, half * 512:(half + 1) * 512],
                               tkc == 0, tkc == 31,
                               waits=expw + vqw + (cxp_free if tkc == 0 else ()))
                            mm(zp[half][0:16, :], selb[:, h * NH:(h + 1) * NH],
                               pT[par][:, half * 512:(half + 1) * 512],
                               (h == 0 and tkc == 0), (h == NH - 1 and tkc == 31),
                               waits=expw)
                        pT_free[par] = Wl('pe')
                    kh_free[khs] = Wl('pe')
                    if h % 4 == 3:
                        vq_free = Wl('pe')
                    cw2 = Wl('pe')
                    ev('vector', lambda e, h=h: e.tensor_copy(
                        ctx[:, h * T:(h + 1) * T], cxp[:, :]), cw2, 'dve', 1)
                    cxp_free = Wl('dve')
                zw = Wl('pe')
                for half in range(2):
                    ev('vector', lambda e, hf=half: e.tensor_copy(
                        zsb[0:16, hf * 512:(hf + 1) * 512], zp[hf][0:16, :]), zw, 'dve', 1)
                drain('vector')
                ev('vector', lambda e: e.reciprocal(zin[0:16, :], zsb[0:16, :]), (), 'dve', 1)
            barrier()
            with contextlib.ExitStack() as PN, contextlib.ExitStack() as SN:
                PP = PN.enter_context
                bsel = SN.enter_context(nc.sbuf_tensor(U("bsel"), [16, 16 * 128], F32))
                dma('gpsimd', bsel[:, :], bselI[:, :], 'dg')
                bselw = Wl('dg')
                bcp = [PP(nc.psum_tensor(U(f"bcp{i}"), [128, T], F32)) for i in range(2)]
                bfree = [(), ()]
                for h in range(NH):
                    par = h % 2
                    for half in range(2):
                        mm(bcp[par][:, half * 512:(half + 1) * 512],
                           bsel[0:16, h * 128:(h + 1) * 128],
                           zin[0:16, half * 512:(half + 1) * 512], True, True,
                           waits=bfree[par] + bselw)
                    ev('vector', lambda e, h=h, p=par: e.tensor_tensor(
                        ctx[:, h * T:(h + 1) * T], ctx[:, h * T:(h + 1) * T],
                        bcp[p][:, :], ALU.mult), Wl('pe'), 'dve', 1)
                    bfree[par] = Wl('dve')
            barrier()
            for j in range(8):
                dma('gpsimd', ag2i[j * 128:(j + 1) * 128, :],
                    ctx[:, (8 + j) * T:(9 + j) * T], 'dg')
            ev('gpsimd', lambda e: e.collective_compute(
                "AllGather", ALU.bypass, replica_groups=RG2, ins=[ag2i[:, :]],
                outs=[ag2o[:, :]]), Wl('dg', 'gpe'), 'col', 1)
            ag2w = Wl('col')
            with contextlib.ExitStack() as SB:
                EB = SB.enter_context
                b0 = [EB(nc.sbuf_tensor(f"b0{i}", [128, T], BF16)) for i in range(2)]
                b1 = [EB(nc.sbuf_tensor(f"b1{i}", [128, T], BF16)) for i in range(2)]
                tmb = [EB(nc.sbuf_tensor(f"tmb{i}", [128, T], BF16)) for i in range(2)]
                bfree = [(), ()]
                for j in range(8):
                    s_ = j % 2
                    sb0 = 's0' if s_ == 0 else 's2'
                    sb1 = 's1' if s_ == 0 else 's3'
                    dma('sync', b0[s_][:, :], ag2o[j * 128:(j + 1) * 128, :], sb0,
                        waits=ag2w + bfree[s_])
                    dma('sync', b1[s_][:, :], ag2o[T + j * 128:T + (j + 1) * 128, :], sb1,
                        waits=ag2w + bfree[s_])
                    lw = Wl(sb0, sb1)
                    ev('vector', lambda e, s=s_: e.tensor_scalar(
                        tmb[s][:, :], b0[s][:, :], uv[:, 1:2], None, ALU.mult), lw, 'dve', 1)
                    drain('vector')
                    ev('vector', lambda e, j=j, s=s_: e.scalar_tensor_tensor(
                        ctx[:, (8 + j) * T:(9 + j) * T], b1[s][:, :], uv[:, 0:1],
                        tmb[s][:, :], ALU.mult, ALU.add), (), 'dve', 1)
                    bfree[s_] = Wl('dve')
            barrier()
            with contextlib.ExitStack() as SP, contextlib.ExitStack() as PP2:
                EP, PP = SP.enter_context, PP2.enter_context
                wpq = EP(nc.sbuf_tensor("wpq", [128, 16 * 512], BF16))
                op = [PP(nc.psum_tensor(f"op{j}", [128, T], F32)) for j in range(4)]
                wpq_free = ()
                ofree = [(), (), (), ()]
                for dmq in range(4):
                    for fc in range(16):
                        dma('gpsimd', wpq[:, fc * 512:(fc + 1) * 512],
                            wprojT[fc * 128:(fc + 1) * 128, dmq * 512:(dmq + 1) * 512],
                            'dg', waits=wpq_free)
                    lw = Wl('dg')
                    for j in range(4):
                        dmc = dmq * 4 + j
                        for fc in range(16):
                            for half in range(2):
                                mm(op[j][:, half * 512:(half + 1) * 512],
                                   wpq[:, fc * 512 + j * 128:fc * 512 + (j + 1) * 128],
                                   ctx[:, fc * T + half * 512:fc * T + (half + 1) * 512],
                                   fc == 0, fc == 15,
                                   waits=lw + (ofree[j] if fc == 0 else ()))
                        ew = Wl('pe')
                        ev('vector', lambda e, j=j, dmc=dmc: e.scalar_tensor_tensor(
                            x[:, dmc * T:(dmc + 1) * T], op[j][:, :], bpv[:, dmc:dmc + 1],
                            x[:, dmc * T:(dmc + 1) * T], ALU.add, ALU.add), ew, 'dve', 1)
                        ofree[j] = Wl('dve')
                    wpq_free = Wl('pe')
            barrier()

        with contextlib.ExitStack() as SX:
            xn3 = SX.enter_context(nc.sbuf_tensor(U("xn3"), [128, DC * T], BF16))
            ln_phase(xn3)
            ffn_pass(xn3)

        for dc in range(DC):
            dma('sync', outT[dc * 128:(dc + 1) * 128, :], x[:, dc * T:(dc + 1) * T], 'ds')
        final = Wl('ds', 'dg', 'k0', 'k1', 'g0', 'g1', 'q0', 'q1', 'v0', 'v1')
        ev('sync', None, final)
        ev('gpsimd', None, final)

        with nc.Block() as block:
            def runner(name):
                def _(e):
                    seen = {}
                    for eng, waits, inc, amt, fn in EV:
                        if eng != name:
                            continue
                        for (s, v) in waits:
                            if v > 0 and seen.get(s, -1) < v:
                                e.wait_ge(sem[s], v)
                                seen[s] = v
                        if fn is None:
                            continue
                        ins = fn(e)
                        if inc:
                            ins.then_inc(sem[inc], amt)
                return _
            block.sync(runner('sync'))
            block.gpsimd(runner('gpsimd'))
            block.vector(runner('vector'))
            block.scalar(runner('scalar'))
            block.tensor(runner('tensor'))
    return nc


_NC_CACHE = [None]


def kernel(L0_attn_query, L0_attn_key, L0_attn_value, L0_attn_Wqkv, L0_attn_Wproj,
           L0_attn_bproj, L0_ffn_w1, L0_ffn_w3, L0_ffn_w2):
    xq = np.ascontiguousarray(np.asarray(L0_attn_query, np.float32))
    Wqkv = np.asarray(L0_attn_Wqkv, np.float32)
    Wproj = np.asarray(L0_attn_Wproj, np.float32)
    bproj = np.ascontiguousarray(np.asarray(L0_attn_bproj, np.float32).reshape(1, D))
    w1 = np.asarray(L0_ffn_w1, np.float32)
    w3 = np.asarray(L0_ffn_w3, np.float32)
    w2 = np.asarray(L0_ffn_w2, np.float32)

    w1T = np.ascontiguousarray(w1.T)
    w3T = np.ascontiguousarray(w3.T)
    w2T = np.ascontiguousarray(w2.T)
    wprojF = np.ascontiguousarray(Wproj.T).reshape(16, 128, D)
    qkv4 = Wqkv.reshape(NH, 3, 128, D)
    _bsel = np.zeros((16, 16 * 128), np.float32)
    for _h in range(16):
        _bsel[_h, _h * 128:(_h + 1) * 128] = 1.0
    in_maps = []
    for c in range(8):
        b, sl = c // 4, c % 4
        keep = list(range(b % 2, 16, 2))
        send = list(range((b + 1) % 2, 16, 2))
        perm = keep + send
        wqkvT_c = np.ascontiguousarray(qkv4[perm].reshape(3 * D, D).T)
        src = [(j if b == 0 else (j + 8 if j < 8 else j - 8)) for j in range(16)]
        wprojT_c = np.ascontiguousarray(wprojF[src].reshape(D, D))
        sl2r0 = c * 688
        sl2_c = np.zeros((768, D), np.float32)
        take = w2T[sl2r0:min(H, sl2r0 + 688), :]
        sl2_c[:take.shape[0], :] = take
        in_maps.append({
            "xT": np.ascontiguousarray(xq[b, sl * T:(sl + 1) * T, :].T),
            "w1T": w1T, "w3T": w3T, "w2T": w2T,
            "wqkvT": wqkvT_c, "wprojT": wprojT_c, "bprojI": bproj,
            "sl1": np.ascontiguousarray(w1T[c * 256:(c + 1) * 256, :]),
            "sl3": np.ascontiguousarray(w3T[c * 256:(c + 1) * 256, :]),
            "sl2": sl2_c,
            "isupI": np.array([[1.0, 0.0]] if c < 4 else [[0.0, 1.0]], np.float32),
            "bselI": _bsel,
        })
    if _NC_CACHE[0] is None:
        _NC_CACHE[0] = build_nc()
    nc = _NC_CACHE[0]
    res = run_bass_kernel_spmd(nc, in_maps, core_ids=list(range(8)))
    out = np.empty((2, S, D), np.float32)
    for c in range(8):
        b, sl = c // 4, c % 4
        out[b, sl * T:(sl + 1) * T, :] = res.results[c]["outT"].T
    return out


# version 02 (final)
# speedup vs baseline: 1.0848x; 1.0848x over previous
import contextlib
import numpy as np
import concourse.bass as bass
import concourse.mybir as mybir
from concourse.bass_utils import run_bass_kernel_spmd

F32 = mybir.dt.float32
BF16 = mybir.dt.bfloat16
ALU = mybir.AluOpType
ACT = mybir.ActivationFunctionType
AX = mybir.AxisListType

D = 2048
DC = 16
T = 1024
H = 5461
HC = 43
NH = 16
S = 4096
NW = float(H * D)
INVSQ = float(1.0 / np.sqrt(128.0))

def hsz(hc):
    return H - 128 * (HC - 1) if hc == HC - 1 else 128

SEMS = ['pe', 'act', 'dve', 'gpe', 'ds', 'dg', 'col', 's0', 's1', 's2', 's3', 'k0', 'k1', 'g0', 'g1', 'q0', 'q1', 'v0', 'v1']


def build_nc():
    nc = bass.Bass("TRN2", target_bir_lowering=False)
    xT = nc.dram_tensor("xT", [D, T], F32, kind="ExternalInput")
    w1T = nc.dram_tensor("w1T", [D, H], F32, kind="ExternalInput")
    w3T = nc.dram_tensor("w3T", [D, H], F32, kind="ExternalInput")
    w2T = nc.dram_tensor("w2T", [H, D], F32, kind="ExternalInput")
    wqkvT = nc.dram_tensor("wqkvT", [D, 3 * D], F32, kind="ExternalInput")
    wprojT = nc.dram_tensor("wprojT", [D, D], F32, kind="ExternalInput")
    bprojI = nc.dram_tensor("bprojI", [1, D], F32, kind="ExternalInput")
    sl1 = nc.dram_tensor("sl1", [256, H], F32, kind="ExternalInput")
    sl3 = nc.dram_tensor("sl3", [256, H], F32, kind="ExternalInput")
    sl2 = nc.dram_tensor("sl2", [768, D], F32, kind="ExternalInput")
    isupI = nc.dram_tensor("isupI", [1, 2], F32, kind="ExternalInput")
    bselI = nc.dram_tensor("bselI", [16, 16 * 128], F32, kind="ExternalInput")
    outT = nc.dram_tensor("outT", [D, T], F32, kind="ExternalOutput")
    q1s = nc.dram_tensor("q1s", [D, H], BF16)
    q3s = nc.dram_tensor("q3s", [D, H], BF16)
    q2s = nc.dram_tensor("q2s", [H, D], BF16)
    ag1i = nc.dram_tensor("ag1i", [S, T], BF16)
    ag1o = nc.dram_tensor("ag1o", [4 * S, T], BF16)
    ag2i = nc.dram_tensor("ag2i", [T, T], BF16)
    ag2o = nc.dram_tensor("ag2o", [2 * T, T], BF16)
    ari = nc.dram_tensor("ari", [1, 4], F32)
    aro = nc.dram_tensor("aro", [1, 4], F32)
    RG4 = [[0, 1, 2, 3], [4, 5, 6, 7]]
    RG2 = [[0, 4], [1, 5], [2, 6], [3, 7]]
    RG8 = [list(range(8))]

    EV = []
    CNT = {s: 0 for s in SEMS}
    BAR = [()]
    UID = [0]

    def U(n):
        UID[0] += 1
        return f"{n}_{UID[0]}" 

    def ev(eng, fn, waits=(), inc=None, amt=1):
        EV.append((eng, BAR[0] + tuple(waits), inc, amt if inc else 0, fn))
        if inc:
            CNT[inc] += amt

    def Wl(*ss):
        return tuple((s, CNT[s]) for s in ss)

    def barrier():
        BAR[0] = Wl('pe', 'act', 'dve', 'gpe', 'ds', 'dg', 'col', 's0', 's1', 's2', 's3', 'k0', 'k1', 'g0', 'g1', 'q0', 'q1', 'v0', 'v1')

    def dma(q, o, i, semn, waits=()):
        ev(q, (lambda e, o=o, i=i: e.dma_start(out=o, in_=i)), waits, semn, 16)

    def drain(eng):
        ev(eng, lambda e: e.drain())

    def mm(ps, lhsT, rhs, start, stop, waits=()):
        ev('tensor', (lambda e, ps=ps, l=lhsT, r=rhs, a=start, b=stop:
                      e.matmul(ps, l, r, start=a, stop=b)), waits, 'pe', 1)

    with contextlib.ExitStack() as ST:
        E = ST.enter_context
        x = E(nc.sbuf_tensor("x", [128, DC * T], F32))
        ones_c_b = E(nc.sbuf_tensor("ones_c_b", [128, 1], BF16))
        ones_r_f = E(nc.sbuf_tensor("ones_r_f", [1, 128], F32))
        selb = E(nc.sbuf_tensor("selb", [128, NH * NH], BF16))
        b20v = E(nc.sbuf_tensor("b20v", [128, 1], F32))
        isqv = E(nc.sbuf_tensor("isqv", [128, 1], F32))
        epsr = E(nc.sbuf_tensor("epsr", [1, 1], F32))
        arsb = E(nc.sbuf_tensor("arsb", [1, 4], F32))
        throw = E(nc.sbuf_tensor("throw", [1, 8], F32))
        thrv = E(nc.sbuf_tensor("thrv", [128, 8], F32))
        uv = E(nc.sbuf_tensor("uv", [128, 2], F32))
        isup_sb = E(nc.sbuf_tensor("isup_sb", [1, 2], F32))
        bpv = E(nc.sbuf_tensor("bpv", [128, DC], F32))
        onesf = E(nc.sbuf_tensor("onesf", [128, 1], F32))
        sem = {s: E(nc.semaphore(s)) for s in SEMS}

        for dc in range(DC):
            dma('sync', x[:, dc * T:(dc + 1) * T], xT[dc * 128:(dc + 1) * 128, :], 'ds')
        for fn in [lambda e: e.memset(ones_c_b[:, :], 1.0),
                   lambda e: e.memset(ones_r_f[0:1, :], 1.0),
                   lambda e: e.memset(onesf[:, :], 1.0),
                   lambda e: e.memset(b20v[:, :], -20.0),
                   lambda e: e.memset(isqv[:, :], INVSQ),
                   lambda e: e.memset(epsr[0:1, :], 1e-5),
                   lambda e: e.memset(selb[:, :], 0.0)]:
            ev('gpsimd', fn, (), 'gpe', 1)
        zdone = Wl('gpe')
        for h in range(NH):
            ev('gpsimd', lambda e, h=h: e.memset(selb[:, h * NH + h:h * NH + h + 1], 1.0),
               zdone, 'gpe', 1)
        for dc in range(DC):
            dma('gpsimd', bpv[:, dc:dc + 1],
                bprojI[0:1, dc * 128:(dc + 1) * 128].rearrange("a p -> p a"), 'k0')
        dma('gpsimd', isup_sb[0:1, :], isupI[0:1, :], 'k0')

        with contextlib.ExitStack() as SC, contextlib.ExitStack() as PC:
            E0, P0 = SC.enter_context, PC.enter_context
            stg = [E0(nc.sbuf_tensor(f"stg{i}", [128, H], F32)) for i in range(2)]
            pa = E0(nc.sbuf_tensor("pa", [128, 10], F32))
            pas = E0(nc.sbuf_tensor("pas", [128, 3], F32))
            psS = P0(nc.psum_tensor("psS", [128, 512], F32))
            loads = ([(sl1, i * 128, H, i) for i in range(2)] +
                     [(sl3, i * 128, H, 2 + i) for i in range(2)] +
                     [(sl2, i * 128, D, 4 + i) for i in range(6)])
            free_at = [(), ()]
            for src, r0, w, col in loads:
                slot = col % 2
                ssem = 's0' if slot == 0 else 's1'
                dma('sync', stg[slot][:, 0:w], src[r0:r0 + 128, 0:w], ssem, waits=free_at[slot])
                ev('vector', lambda e, c=col, s_=slot, w=w:
                   e.reduce_sum(pa[:, c:c + 1], stg[s_][:, 0:w], axis=AX.X,
                                apply_absolute_value=True), Wl(ssem), 'dve', 1)
                free_at[slot] = Wl('dve')
            drain('vector')
            ev('vector', lambda e: e.reduce_sum(pas[:, 0:1], pa[:, 0:2], axis=AX.X), (), 'dve', 1)
            ev('vector', lambda e: e.reduce_sum(pas[:, 1:2], pa[:, 2:4], axis=AX.X), (), 'dve', 1)
            ev('vector', lambda e: e.reduce_sum(pas[:, 2:3], pa[:, 4:10], axis=AX.X), (), 'dve', 1)
            drain('vector')
            pvw = Wl('dve', 'gpe')
            for j in range(3):
                mm(psS[0:1, j:j + 1], onesf[:, :], pas[:, j:j + 1], True, True, waits=pvw)
            ev('vector', lambda e: e.tensor_copy(arsb[0:1, 0:3], psS[0:1, 0:3]), Wl('pe'), 'dve', 1)
            ev('vector', lambda e: e.memset(arsb[0:1, 3:4], 0.0), (), 'dve', 1)
            dma('gpsimd', ari[0:1, :], arsb[0:1, :], 'dg', waits=Wl('dve'))
            ev('gpsimd', lambda e: e.collective_compute(
                "AllReduce", ALU.add, replica_groups=RG8, ins=[ari[:, :]], outs=[aro[:, :]]),
                Wl('dg'), 'col', 1)
            dma('gpsimd', arsb[0:1, :], aro[0:1, :], 'dg', waits=Wl('col'))
            arw = Wl('dg')
            for j, (c0, fac) in enumerate([(0, 0.5 / NW), (0, -0.5 / NW), (1, 0.5 / NW),
                                           (1, -0.5 / NW), (2, 0.5 / NW), (2, -0.5 / NW),
                                           (0, 1.0 / NW)]):
                ev('vector', lambda e, j=j, c0=c0, fac=fac:
                   e.tensor_scalar(throw[0:1, j:j + 1], arsb[0:1, c0:c0 + 1], fac, None, ALU.mult),
                   arw, 'dve', 1)
            ev('vector', lambda e: e.scalar_tensor_tensor(
                throw[0:1, 7:8], arsb[0:1, 1:2], 0.5 / (NW * NW), arsb[0:1, 2:3],
                ALU.mult, ALU.mult), arw, 'dve', 1)
            drain('vector')
            tw = Wl('dve')
            mm(psS[0:128, 0:8], ones_r_f[0:1, :], throw[0:1, 0:8], True, True, waits=tw)
            mm(psS[0:128, 8:10], ones_r_f[0:1, :], isup_sb[0:1, 0:2], True, True,
               waits=tw + Wl('k0'))
            ev('vector', lambda e: e.tensor_copy(thrv[:, :], psS[:, 0:8]), Wl('pe'), 'dve', 1)
            ev('vector', lambda e: e.tensor_copy(uv[:, :], psS[:, 8:10]), Wl('pe'), 'dve', 1)
        barrier()

        def ln_phase(xn):
            with contextlib.ExitStack() as SC, contextlib.ExitStack() as PC:
                E0, P0 = SC.enter_context, PC.enter_context
                xc = [E0(nc.sbuf_tensor(U(f"lnxc{i}"), [128, T], BF16)) for i in range(2)]
                sq = [E0(nc.sbuf_tensor(U(f"lnsq{i}"), [128, T], BF16)) for i in range(2)]
                tmp = [E0(nc.sbuf_tensor(U(f"lntmp{i}"), [128, T], F32)) for i in range(2)]
                rowm = E0(nc.sbuf_tensor(U("rowm"), [1, T], F32))
                rowv = E0(nc.sbuf_tensor(U("rowv"), [1, T], F32))
                rowt = E0(nc.sbuf_tensor(U("rowt"), [1, T], F32))
                rowA = E0(nc.sbuf_tensor(U("rowA"), [1, T], F32))
                rowB = E0(nc.sbuf_tensor(U("rowB"), [1, T], F32))
                ps_sx = P0(nc.psum_tensor(U("ps_sx"), [128, T], F32))
                ps_s2 = P0(nc.psum_tensor(U("ps_s2"), [128, T], F32))
                bcA = P0(nc.psum_tensor(U("bcA"), [128, T], F32))
                bcB = P0(nc.psum_tensor(U("bcB"), [128, T], F32))
                xc_free = [(), ()]
                sq_free = [(), ()]
                for dc in range(DC):
                    s_ = dc % 2
                    ev('vector', lambda e, dc=dc, s=s_: e.tensor_copy(
                        xc[s][:, :], x[:, dc * T:(dc + 1) * T]), xc_free[s_], 'dve', 1)
                    cw = Wl('dve')
                    ev('scalar', lambda e, s=s_: e.activation(
                        out=sq[s][:, :], in_=xc[s][:, :], func=ACT.Square, bias=0.0, scale=1.0),
                        cw + sq_free[s_], 'act', 1)
                    aw = Wl('act')
                    for half in range(2):
                        mm(ps_sx[0:1, half * 512:(half + 1) * 512], ones_c_b[:, :],
                           xc[s_][:, half * 512:(half + 1) * 512], dc == 0, dc == DC - 1, waits=cw)
                        mm(ps_s2[0:1, half * 512:(half + 1) * 512], ones_c_b[:, :],
                           sq[s_][:, half * 512:(half + 1) * 512], dc == 0, dc == DC - 1, waits=aw)
                    xc_free[s_] = Wl('pe') + aw
                    sq_free[s_] = Wl('pe')
                statw = Wl('pe')
                ev('vector', lambda e: e.tensor_scalar(rowm[0:1, :], ps_sx[0:1, :],
                   1.0 / D, None, ALU.mult), statw, 'dve', 1)
                ev('scalar', lambda e: e.activation(out=rowt[0:1, :], in_=rowm[0:1, :],
                   func=ACT.Square, bias=0.0, scale=1.0), Wl('dve'), 'act', 1)
                ev('vector', lambda e: e.scalar_tensor_tensor(rowv[0:1, :], ps_s2[0:1, :],
                   1.0 / D, rowt[0:1, :], ALU.mult, ALU.subtract), Wl('act'), 'dve', 1)
                ev('scalar', lambda e: e.activation(out=rowt[0:1, :], in_=rowv[0:1, :],
                   func=ACT.Sqrt, bias=epsr[0:1, :], scale=1.0), Wl('dve'), 'act', 1)
                ev('vector', lambda e: e.reciprocal(rowA[0:1, :], rowt[0:1, :]), Wl('act'), 'dve', 1)
                drain('vector')
                ev('vector', lambda e: e.scalar_tensor_tensor(rowB[0:1, :], rowm[0:1, :],
                   -1.0, rowA[0:1, :], ALU.mult, ALU.mult), (), 'dve', 1)
                drain('vector')
                rw = Wl('dve')
                for half in range(2):
                    mm(bcA[:, half * 512:(half + 1) * 512], ones_r_f[0:1, :],
                       rowA[0:1, half * 512:(half + 1) * 512], True, True, waits=rw)
                    mm(bcB[:, half * 512:(half + 1) * 512], ones_r_f[0:1, :],
                       rowB[0:1, half * 512:(half + 1) * 512], True, True, waits=rw)
                bw = Wl('pe')
                tmp_free = [(), ()]
                for dc in range(DC):
                    s_ = dc % 2
                    ev('vector', lambda e, dc=dc, s=s_: e.tensor_tensor(
                        tmp[s][:, :], x[:, dc * T:(dc + 1) * T], bcA[:, :], ALU.mult),
                        bw + tmp_free[s_], 'dve', 1)
                    drain('vector')
                    ev('vector', lambda e, dc=dc, s=s_: e.tensor_tensor(
                        xn[:, dc * T:(dc + 1) * T], tmp[s][:, :], bcB[:, :], ALU.add),
                        (), 'dve', 1)
                    tmp_free[s_] = Wl('dve')
            barrier()

        def quant_phase(srcT, dstS, rows, cols, cp, cn):
            ntile = (rows + 127) // 128
            with contextlib.ExitStack() as SC:
                E0 = SC.enter_context
                stg = [E0(nc.sbuf_tensor(U(f"qstg{i}"), [128, cols], F32)) for i in range(2)]
                qb = E0(nc.sbuf_tensor(U("qqb"), [128, cols], BF16))
                qr = [E0(nc.sbuf_tensor(U(f"qqr{i}"), [128, cols], BF16)) for i in range(2)]
                free_at = [(), ()]
                wr_at = [(), ()]
                for i in range(ntile):
                    s_ = i % 2
                    ssem = 's0' if s_ == 0 else 's1'
                    r0 = i * 128
                    rh = min(128, rows - r0)
                    dma('sync', stg[s_][0:rh, :], srcT[r0:r0 + rh, :], ssem, waits=free_at[s_])
                    lw = Wl(ssem)
                    ev('vector', lambda e, s=s_, rh=rh: e.tensor_scalar(
                        qb[0:rh, :], stg[s][0:rh, :], thrv[0:rh, cn:cn + 1], None, ALU.is_le),
                        lw + wr_at[s_], 'dve', 1)
                    drain('vector')
                    ev('vector', lambda e, s=s_, rh=rh: e.scalar_tensor_tensor(
                        qr[s][0:rh, :], stg[s][0:rh, :], thrv[0:rh, cp:cp + 1], qb[0:rh, :],
                        ALU.is_ge, ALU.subtract), (), 'dve', 1)
                    drain('vector')
                    free_at[s_] = Wl('dve')
                    wsem = 'g0' if s_ == 0 else 'g1'
                    dma('gpsimd', dstS[r0:r0 + rh, :], qr[s_][0:rh, :], wsem, waits=Wl('dve'))
                    wr_at[s_] = Wl(wsem)
            barrier()

        def ffn_pass(xn):
          for _hl, _hh in ((0, 22), (22, HC)):
            ffn_half(xn, _hl, _hh)

        def ffn_half(xn, hc_lo, hc_hi):
            NBL = (hc_hi - hc_lo + 1) // 2
            with contextlib.ExitStack() as SG:
                g = SG.enter_context(nc.sbuf_tensor(U("g"), [128, 22 * T], BF16))
                with contextlib.ExitStack() as SC, contextlib.ExitStack() as PC:
                    E0, P0 = SC.enter_context, PC.enter_context
                    ring = [E0(nc.sbuf_tensor(U(f"ring{i}"), [128, DC * 256], BF16)) for i in range(3)]
                    sis = E0(nc.sbuf_tensor(U("sis"), [128, T], BF16))
                    h1p = [P0(nc.psum_tensor(U(f"h1p{j}"), [128, T], F32)) for j in range(2)]
                    h3p = [P0(nc.psum_tensor(U(f"h3p{j}"), [128, T], F32)) for j in range(2)]
                    blk_free = [(), (), ()]
                    h1_free = [(), ()]
                    h3_free = [(), ()]
                    sis_free = ()
                    rsem = ['s0', 's1', 's2']
                    for bi in range(NBL):
                        r1, r3 = (2 * bi) % 3, (2 * bi + 1) % 3
                        hcs = [hc for hc in (hc_lo + 2 * bi, hc_lo + 2 * bi + 1) if hc < hc_hi]
                        c0 = hcs[0] * 128
                        cw_ = sum(hsz(h) for h in hcs)
                        dma('sync', ring[r1].rearrange("p (a c) -> p a c", c=256)[:, :, 0:cw_],
                            q1s[:, c0:c0 + cw_].rearrange("(a p) c -> p a c", p=128),
                            rsem[r1], waits=blk_free[r1])
                        dma('sync', ring[r3].rearrange("p (a c) -> p a c", c=256)[:, :, 0:cw_],
                            q3s[:, c0:c0 + cw_].rearrange("(a p) c -> p a c", p=128),
                            rsem[r3], waits=blk_free[r3])
                        lw = Wl(rsem[r1], rsem[r3])
                        for dc in range(DC):
                            for j, hc in enumerate(hcs):
                                co = dc * 256 + (hsz(hcs[0]) if j else 0)
                                sz = hsz(hc)
                                for half in range(2):
                                    rhs = xn[:, dc * T + half * 512:dc * T + (half + 1) * 512]
                                    mm(h1p[j][0:sz, half * 512:(half + 1) * 512],
                                       ring[r1][:, co:co + sz], rhs, dc == 0, dc == DC - 1,
                                       waits=lw + (h1_free[j] if dc == 0 else ()))
                                    mm(h3p[j][0:sz, half * 512:(half + 1) * 512],
                                       ring[r3][:, co:co + sz], rhs, dc == 0, dc == DC - 1,
                                       waits=lw + (h3_free[j] if dc == 0 else ()))
                        mmw = Wl('pe')
                        for j, hc in enumerate(hcs):
                            sz = hsz(hc)
                            ev('scalar', lambda e, j=j, sz=sz: e.activation(
                                out=sis[0:sz, :], in_=h1p[j][0:sz, :], func=ACT.Silu,
                                bias=0.0, scale=thrv[0:sz, 6:7]), mmw + sis_free, 'act', 1)
                            h1_free[j] = Wl('act')
                            ev('vector', lambda e, j=j, gl=hc - hc_lo, sz=sz: e.tensor_tensor(
                                g[0:sz, gl * T:(gl + 1) * T], sis[0:sz, :], h3p[j][0:sz, :],
                                ALU.mult), Wl('act'), 'dve', 1)
                            h3_free[j] = Wl('dve')
                            sis_free = Wl('dve')
                        blk_free[r1] = Wl('pe')
                        blk_free[r3] = Wl('pe')
                    gw = Wl('dve')
                with contextlib.ExitStack() as SC2, contextlib.ExitStack() as PC2:
                    E2, P2 = SC2.enter_context, PC2.enter_context
                    q2b = [E2(nc.sbuf_tensor(U(f"q2b{i}"), [128, 512], BF16)) for i in range(2)]
                    yp = [P2(nc.psum_tensor(U(f"yp{j}"), [128, T], F32)) for j in range(4)]
                    q2_free = [(), ()]
                    yfree = [(), (), (), ()]
                    for dmq in range(4):
                        for hc in range(hc_lo, hc_hi):
                            s_ = hc % 2
                            ssem = 's0' if s_ == 0 else 's1'
                            sz = hsz(hc)
                            gl = hc - hc_lo
                            dma('sync', q2b[s_][0:sz, :],
                                q2s[hc * 128:hc * 128 + sz, dmq * 512:(dmq + 1) * 512],
                                ssem, waits=q2_free[s_] + gw)
                            lw2 = Wl(ssem)
                            for j in range(4):
                                for half in range(2):
                                    mm(yp[j][:, half * 512:(half + 1) * 512],
                                       q2b[s_][0:sz, j * 128:(j + 1) * 128],
                                       g[0:sz, gl * T + half * 512:gl * T + (half + 1) * 512],
                                       hc == hc_lo, hc == hc_hi - 1,
                                       waits=lw2 + (yfree[j] if hc == hc_lo else ()))
                            q2_free[s_] = Wl('pe')
                        ew = Wl('pe')
                        for j in range(4):
                            dmc = dmq * 4 + j
                            ev('vector', lambda e, j=j, dmc=dmc: e.scalar_tensor_tensor(
                                x[:, dmc * T:(dmc + 1) * T], yp[j][:, :], thrv[:, 7:8],
                                x[:, dmc * T:(dmc + 1) * T], ALU.mult, ALU.add), ew, 'dve', 1)
                            yfree[j] = Wl('dve')
            barrier()

        with contextlib.ExitStack() as SX:
            xn1 = SX.enter_context(nc.sbuf_tensor(U("xn1"), [128, DC * T], BF16))
            ln_phase(xn1)
            quant_phase(w1T, q1s, D, H, 0, 1)
            quant_phase(w3T, q3s, D, H, 2, 3)
            quant_phase(w2T, q2s, H, D, 4, 5)
            ffn_pass(xn1)

        with contextlib.ExitStack() as SA:
            EA = SA.enter_context
            q_all = EA(nc.sbuf_tensor("q_all", [128, NH * T], BF16))
            with contextlib.ExitStack() as SX:
                xn2 = SX.enter_context(nc.sbuf_tensor(U("xn2"), [128, DC * T], BF16))
                ln_phase(xn2)
                with contextlib.ExitStack() as SQ, contextlib.ExitStack() as PQ:
                    EQ, PP = SQ.enter_context, PQ.enter_context
                    qkw = [EQ(nc.sbuf_tensor(f"qkw{i}", [128, DC * 256], BF16)) for i in range(2)]
                    kst = [EQ(nc.sbuf_tensor(f"kst{i}", [128, T], BF16)) for i in range(2)]
                    qp = [PP(nc.psum_tensor(f"qp{i}", [128, T], F32)) for i in range(2)]
                    kp = [PP(nc.psum_tensor(f"kp{i}", [128, T], F32)) for i in range(2)]
                    qkw_free = [(), ()]
                    qp_free = [(), ()]
                    kp_free = [(), ()]
                    kst_free = [(), ()]
                    for h in range(NH):
                        s_ = h % 2
                        qsem = 'q0' if s_ == 0 else 'q1'
                        dma('gpsimd', qkw[s_].rearrange("p (a c) -> p a c", c=256),
                            wqkvT[:, h * 384:h * 384 + 256].rearrange("(a p) c -> p a c", p=128),
                            qsem, waits=qkw_free[s_])
                        lw = Wl(qsem)
                        for dc in range(DC):
                            for half in range(2):
                                rhs = xn2[:, dc * T + half * 512:dc * T + (half + 1) * 512]
                                mm(qp[s_][:, half * 512:(half + 1) * 512],
                                   qkw[s_][:, dc * 256:dc * 256 + 128], rhs,
                                   dc == 0, dc == DC - 1,
                                   waits=lw + (qp_free[s_] if dc == 0 else ()))
                                mm(kp[s_][:, half * 512:(half + 1) * 512],
                                   qkw[s_][:, dc * 256 + 128:dc * 256 + 256], rhs,
                                   dc == 0, dc == DC - 1,
                                   waits=lw + (kp_free[s_] if dc == 0 else ()))
                        qkw_free[s_] = Wl('pe')
                        mmw = Wl('pe')
                        ev('scalar', lambda e, h=h, s=s_: e.activation(
                            out=q_all[:, h * T:(h + 1) * T], in_=qp[s][:, :], func=ACT.Copy,
                            bias=0.0, scale=isqv[:, :]), mmw, 'act', 1)
                        qp_free[s_] = Wl('act')
                        ev('scalar', lambda e, s=s_: e.activation(
                            out=kst[s][:, :], in_=kp[s][:, :], func=ACT.Copy,
                            bias=0.0, scale=1.0), mmw + kst_free[s_], 'act', 1)
                        kp_free[s_] = Wl('act')
                        ksem = 'k0' if s_ == 0 else 'k1'
                        dma('gpsimd', ag1i[h * 128:(h + 1) * 128, :], kst[s_][:, :], ksem,
                            waits=Wl('act'))
                        kst_free[s_] = Wl(ksem)
                barrier()
                ag1v = ag1i[D:S, :].rearrange("(a b) c -> a (b c)", b=2)
                with contextlib.ExitStack() as SV, contextlib.ExitStack() as PV:
                    EQ, PP = SV.enter_context, PV.enter_context
                    vw = [EQ(nc.sbuf_tensor(f"vw{i}", [128, DC * 512], BF16)) for i in range(2)]
                    vst = [EQ(nc.sbuf_tensor(f"vst{i}", [128, 512], BF16)) for i in range(2)]
                    vp = [PP(nc.psum_tensor(f"vp{t}", [128, 512], F32)) for t in range(8)]
                    vw_free = [(), ()]
                    vp_free = [()] * 8
                    vst_free = [(), ()]
                    for qd in range(4):
                        s_ = qd % 2
                        vsem = 'v0' if s_ == 0 else 'v1'
                        for j in range(4):
                            col0 = (4 * qd + j) * 384 + 256
                            dma('gpsimd',
                                vw[s_].rearrange("p (a b) -> p a b", b=512)[:, :, j * 128:(j + 1) * 128],
                                wqkvT[:, col0:col0 + 128].rearrange("(a p) c -> p a c", p=128),
                                vsem, waits=vw_free[s_] if j == 0 else ())
                        lw = Wl(vsem)
                        for dc in range(DC):
                            for tc in range(8):
                                mm(vp[tc][:, :], xn2[:, dc * T + tc * 128:dc * T + (tc + 1) * 128],
                                   vw[s_][:, dc * 512:(dc + 1) * 512], dc == 0, dc == DC - 1,
                                   waits=lw + (vp_free[tc] if dc == 0 else ()))
                        vw_free[s_] = Wl('pe')
                        mmw = Wl('pe')
                        for tc in range(8):
                            ssl = tc % 2
                            ev('scalar', lambda e, tc=tc, s=ssl: e.activation(
                                out=vst[s][:, :], in_=vp[tc][:, :], func=ACT.Copy,
                                bias=0.0, scale=1.0), mmw + vst_free[ssl], 'act', 1)
                            vp_free[tc] = Wl('act')
                            wsem = 'g0' if ssl == 0 else 'g1'
                            dma('gpsimd', ag1v[tc * 128:(tc + 1) * 128, qd * 512:(qd + 1) * 512],
                                vst[ssl][:, :], wsem, waits=Wl('act'))
                            vst_free[ssl] = Wl(wsem)
            ev('gpsimd', lambda e: e.collective_compute(
                "AllGather", ALU.bypass, replica_groups=RG4, ins=[ag1i[:, :]], outs=[ag1o[:, :]]),
                Wl('k0', 'k1', 'g0', 'g1', 'gpe'), 'col', 1)
            ag1w = Wl('col')
            barrier()

            ctx = EA(nc.sbuf_tensor("ctx", [128, NH * T], BF16))
            zsb = EA(nc.sbuf_tensor("zsb", [16, T], F32))
            zin = EA(nc.sbuf_tensor("zin", [16, T], F32))
            with contextlib.ExitStack() as SK, contextlib.ExitStack() as PK:
                EK, PP = SK.enter_context, PK.enter_context
                kh = [EK(nc.sbuf_tensor(f"kh{i}", [128, S], BF16)) for i in range(2)]
                vq = EK(nc.sbuf_tensor("vq", [128, 32 * 512], BF16))
                pT = [EK(nc.sbuf_tensor(f"pT{i}", [128, T], BF16)) for i in range(2)]
                sp = [PP(nc.psum_tensor(f"sp{i}", [128, T], F32)) for i in range(2)]
                cxp = PP(nc.psum_tensor("cxp", [128, T], F32))
                zp = [PP(nc.psum_tensor(f"zp{i}", [128, 512], F32)) for i in range(2)]
                kh_free = [(), ()]
                vq_free = ()
                sp_free = [(), ()]
                cxp_free = ()
                pT_free = [(), ()]
                vqw = ()
                for h in range(NH):
                    khs = h % 2
                    if h % 4 == 0:
                        for tkc in range(32):
                            r, tl = tkc // 8, tkc % 8
                            src = ag1o[r * S + D:(r + 1) * S, :].rearrange("(a b) c -> a (b c)", b=2)
                            dma('sync', vq[:, tkc * 512:(tkc + 1) * 512],
                                src[tl * 128:(tl + 1) * 128, (h // 4) * 512:(h // 4 + 1) * 512],
                                's2', waits=ag1w + vq_free)
                        vqw = Wl('s2')
                    khsem = 's0' if khs == 0 else 's1'
                    for r in range(4):
                        dma('sync', kh[khs][:, r * T:(r + 1) * T],
                            ag1o[r * S + h * 128:r * S + (h + 1) * 128, :], khsem,
                            waits=ag1w + kh_free[khs])
                    khw = Wl(khsem)
                    for tkc in range(32):
                        par = tkc % 2
                        for half in range(2):
                            mm(sp[par][:, half * 512:(half + 1) * 512],
                               kh[khs][:, tkc * 128:(tkc + 1) * 128],
                               q_all[:, h * T + half * 512:h * T + (half + 1) * 512],
                               True, True, waits=khw + sp_free[par])
                        spw = Wl('pe')
                        ev('scalar', lambda e, p=par: e.activation(
                            out=pT[p][:, :], in_=sp[p][:, :], func=ACT.Exp,
                            bias=b20v[:, :], scale=1.0), spw + pT_free[par], 'act', 1)
                        expw = Wl('act')
                        sp_free[par] = expw
                        for half in range(2):
                            mm(cxp[:, half * 512:(half + 1) * 512],
                               vq[:, tkc * 512 + (h % 4) * 128:tkc * 512 + (h % 4 + 1) * 128],
                               pT[par][:, half * 512:(half + 1) * 512],
                               tkc == 0, tkc == 31,
                               waits=expw + vqw + (cxp_free if tkc == 0 else ()))
                            mm(zp[half][0:16, :], selb[:, h * NH:(h + 1) * NH],
                               pT[par][:, half * 512:(half + 1) * 512],
                               (h == 0 and tkc == 0), (h == NH - 1 and tkc == 31),
                               waits=expw)
                        pT_free[par] = Wl('pe')
                    kh_free[khs] = Wl('pe')
                    if h % 4 == 3:
                        vq_free = Wl('pe')
                    cw2 = Wl('pe')
                    ev('vector', lambda e, h=h: e.tensor_copy(
                        ctx[:, h * T:(h + 1) * T], cxp[:, :]), cw2, 'dve', 1)
                    cxp_free = Wl('dve')
                zw = Wl('pe')
                for half in range(2):
                    ev('vector', lambda e, hf=half: e.tensor_copy(
                        zsb[0:16, hf * 512:(hf + 1) * 512], zp[hf][0:16, :]), zw, 'dve', 1)
                drain('vector')
                ev('vector', lambda e: e.reciprocal(zin[0:16, :], zsb[0:16, :]), (), 'dve', 1)
            barrier()
            with contextlib.ExitStack() as PN, contextlib.ExitStack() as SN:
                PP = PN.enter_context
                bsel = SN.enter_context(nc.sbuf_tensor(U("bsel"), [16, 16 * 128], F32))
                dma('gpsimd', bsel[:, :], bselI[:, :], 'dg')
                bselw = Wl('dg')
                bcp = [PP(nc.psum_tensor(U(f"bcp{i}"), [128, T], F32)) for i in range(2)]
                bfree = [(), ()]
                for h in range(NH):
                    par = h % 2
                    for half in range(2):
                        mm(bcp[par][:, half * 512:(half + 1) * 512],
                           bsel[0:16, h * 128:(h + 1) * 128],
                           zin[0:16, half * 512:(half + 1) * 512], True, True,
                           waits=bfree[par] + bselw)
                    ev('vector', lambda e, h=h, p=par: e.tensor_tensor(
                        ctx[:, h * T:(h + 1) * T], ctx[:, h * T:(h + 1) * T],
                        bcp[p][:, :], ALU.mult), Wl('pe'), 'dve', 1)
                    bfree[par] = Wl('dve')
            barrier()
            for j in range(8):
                dma('gpsimd', ag2i[j * 128:(j + 1) * 128, :],
                    ctx[:, (8 + j) * T:(9 + j) * T], 'dg')
            ev('gpsimd', lambda e: e.collective_compute(
                "AllGather", ALU.bypass, replica_groups=RG2, ins=[ag2i[:, :]],
                outs=[ag2o[:, :]]), Wl('dg', 'gpe'), 'col', 1)
            ag2w = Wl('col')
            with contextlib.ExitStack() as SB:
                EB = SB.enter_context
                b0 = [EB(nc.sbuf_tensor(f"b0{i}", [128, T], BF16)) for i in range(2)]
                b1 = [EB(nc.sbuf_tensor(f"b1{i}", [128, T], BF16)) for i in range(2)]
                tmb = [EB(nc.sbuf_tensor(f"tmb{i}", [128, T], BF16)) for i in range(2)]
                bfree = [(), ()]
                for j in range(8):
                    s_ = j % 2
                    sb0 = 's0' if s_ == 0 else 's2'
                    sb1 = 's1' if s_ == 0 else 's3'
                    dma('sync', b0[s_][:, :], ag2o[j * 128:(j + 1) * 128, :], sb0,
                        waits=ag2w + bfree[s_])
                    dma('sync', b1[s_][:, :], ag2o[T + j * 128:T + (j + 1) * 128, :], sb1,
                        waits=ag2w + bfree[s_])
                    lw = Wl(sb0, sb1)
                    ev('vector', lambda e, s=s_: e.tensor_scalar(
                        tmb[s][:, :], b0[s][:, :], uv[:, 1:2], None, ALU.mult), lw, 'dve', 1)
                    drain('vector')
                    ev('vector', lambda e, j=j, s=s_: e.scalar_tensor_tensor(
                        ctx[:, (8 + j) * T:(9 + j) * T], b1[s][:, :], uv[:, 0:1],
                        tmb[s][:, :], ALU.mult, ALU.add), (), 'dve', 1)
                    bfree[s_] = Wl('dve')
            barrier()
            with contextlib.ExitStack() as SP, contextlib.ExitStack() as PP2:
                EP, PP = SP.enter_context, PP2.enter_context
                wpq = EP(nc.sbuf_tensor("wpq", [128, 16 * 512], BF16))
                op = [PP(nc.psum_tensor(f"op{j}", [128, T], F32)) for j in range(4)]
                wpq_free = ()
                ofree = [(), (), (), ()]
                for dmq in range(4):
                    for fc in range(16):
                        dma('gpsimd', wpq[:, fc * 512:(fc + 1) * 512],
                            wprojT[fc * 128:(fc + 1) * 128, dmq * 512:(dmq + 1) * 512],
                            'dg', waits=wpq_free)
                    lw = Wl('dg')
                    for j in range(4):
                        dmc = dmq * 4 + j
                        for fc in range(16):
                            for half in range(2):
                                mm(op[j][:, half * 512:(half + 1) * 512],
                                   wpq[:, fc * 512 + j * 128:fc * 512 + (j + 1) * 128],
                                   ctx[:, fc * T + half * 512:fc * T + (half + 1) * 512],
                                   fc == 0, fc == 15,
                                   waits=lw + (ofree[j] if fc == 0 else ()))
                        ew = Wl('pe')
                        ev('vector', lambda e, j=j, dmc=dmc: e.scalar_tensor_tensor(
                            x[:, dmc * T:(dmc + 1) * T], op[j][:, :], bpv[:, dmc:dmc + 1],
                            x[:, dmc * T:(dmc + 1) * T], ALU.add, ALU.add), ew, 'dve', 1)
                        ofree[j] = Wl('dve')
                    wpq_free = Wl('pe')
            barrier()

        with contextlib.ExitStack() as SX:
            xn3 = SX.enter_context(nc.sbuf_tensor(U("xn3"), [128, DC * T], BF16))
            ln_phase(xn3)
            ffn_pass(xn3)

        for dc in range(DC):
            dma('sync', outT[dc * 128:(dc + 1) * 128, :], x[:, dc * T:(dc + 1) * T], 'ds')
        final = Wl('ds', 'dg', 'k0', 'k1', 'g0', 'g1', 'q0', 'q1', 'v0', 'v1')
        ev('sync', None, final)
        ev('gpsimd', None, final)

        with nc.Block() as block:
            def runner(name):
                def _(e):
                    seen = {}
                    for eng, waits, inc, amt, fn in EV:
                        if eng != name:
                            continue
                        for (s, v) in waits:
                            if v > 0 and seen.get(s, -1) < v:
                                e.wait_ge(sem[s], v)
                                seen[s] = v
                        if fn is None:
                            continue
                        ins = fn(e)
                        if inc:
                            ins.then_inc(sem[inc], amt)
                return _
            block.sync(runner('sync'))
            block.gpsimd(runner('gpsimd'))
            block.vector(runner('vector'))
            block.scalar(runner('scalar'))
            block.tensor(runner('tensor'))
    return nc


_NC_CACHE = [None]


def kernel(L0_attn_query, L0_attn_key, L0_attn_value, L0_attn_Wqkv, L0_attn_Wproj,
           L0_attn_bproj, L0_ffn_w1, L0_ffn_w3, L0_ffn_w2):
    xq = np.ascontiguousarray(np.asarray(L0_attn_query, np.float32))
    Wqkv = np.asarray(L0_attn_Wqkv, np.float32)
    Wproj = np.asarray(L0_attn_Wproj, np.float32)
    bproj = np.ascontiguousarray(np.asarray(L0_attn_bproj, np.float32).reshape(1, D))
    w1 = np.asarray(L0_ffn_w1, np.float32)
    w3 = np.asarray(L0_ffn_w3, np.float32)
    w2 = np.asarray(L0_ffn_w2, np.float32)

    w1T = np.ascontiguousarray(w1.T)
    w3T = np.ascontiguousarray(w3.T)
    w2T = np.ascontiguousarray(w2.T)
    wprojF = np.ascontiguousarray(Wproj.T).reshape(16, 128, D)
    qkv4 = Wqkv.reshape(NH, 3, 128, D)
    _bsel = np.zeros((16, 16 * 128), np.float32)
    for _h in range(16):
        _bsel[_h, _h * 128:(_h + 1) * 128] = 1.0
    in_maps = []
    for c in range(8):
        b, sl = c // 4, c % 4
        keep = list(range(b % 2, 16, 2))
        send = list(range((b + 1) % 2, 16, 2))
        perm = keep + send
        wqkvT_c = np.ascontiguousarray(qkv4[perm].reshape(3 * D, D).T)
        src = [(j if b == 0 else (j + 8 if j < 8 else j - 8)) for j in range(16)]
        wprojT_c = np.ascontiguousarray(wprojF[src].reshape(D, D))
        sl2r0 = c * 688
        sl2_c = np.zeros((768, D), np.float32)
        take = w2T[sl2r0:min(H, sl2r0 + 688), :]
        sl2_c[:take.shape[0], :] = take
        in_maps.append({
            "xT": np.ascontiguousarray(xq[b, sl * T:(sl + 1) * T, :].T),
            "w1T": w1T, "w3T": w3T, "w2T": w2T,
            "wqkvT": wqkvT_c, "wprojT": wprojT_c, "bprojI": bproj,
            "sl1": np.ascontiguousarray(w1T[c * 256:(c + 1) * 256, :]),
            "sl3": np.ascontiguousarray(w3T[c * 256:(c + 1) * 256, :]),
            "sl2": sl2_c,
            "isupI": np.array([[1.0, 0.0]] if c < 4 else [[0.0, 1.0]], np.float32),
            "bselI": _bsel,
        })
    if _NC_CACHE[0] is None:
        _NC_CACHE[0] = build_nc()
    nc = _NC_CACHE[0]
    res = run_bass_kernel_spmd(nc, in_maps, core_ids=list(range(8)))
    out = np.empty((2, S, D), np.float32)
    for c in range(8):
        b, sl = c // 4, c % 4
        out[b, sl * T:(sl + 1) * T, :] = res.results[c]["outT"].T
    return out
